# version 08
# baseline (speedup 1.0000x reference)
import numpy as np
import ml_dtypes
import concourse.bass as bass
import concourse.mybir as mybir

F32 = mybir.dt.float32
BF16 = mybir.dt.bfloat16
I32 = mybir.dt.int32
AF = mybir.ActivationFunctionType
ALU = mybir.AluOpType
AX = mybir.AxisListType.X

H = 16
R1, R2 = 4, 8
SCALE = float((R1 * R2) ** -0.5)
QSCL = float(1.0 / (127.0 * 127.0))
EPS = 1e-5
BF = ml_dtypes.bfloat16


class Sem:
    __slots__ = ("h", "name", "v", "incers")

    def __init__(self, handle, name):
        self.h = handle
        self.name = name
        self.v = 0
        self.incers = set()


class Sched:
    ENGINES = ("sync", "scalar", "vector", "tensor", "gpsimd")

    def __init__(self, nc):
        self.nc = nc
        self.events = {e: [] for e in self.ENGINES}
        self.last_wait = {e: {} for e in self.ENGINES}

    def sem(self, name):
        return Sem(self.nc.alloc_semaphore(name=name), name)

    def op(self, engine, fn, waits=(), inc=None, dma=False):
        amt = 16 if dma else 1
        wl = []
        best = {}
        for s, v in waits:
            if v <= 0:
                continue
            if v > best.get(s.name, (None, -1))[1]:
                best[s.name] = (s, v)
        for s, v in best.values():
            if v <= self.last_wait[engine].get(s.name, -1):
                continue
            self.last_wait[engine][s.name] = v
            wl.append((s, v))
        if inc is not None:
            inc.v += amt
            inc.incers.add(engine)
        self.events[engine].append((wl, fn, inc, amt))
        return inc.v if inc is not None else None

    def emit(self):
        with self.nc.Block() as block:
            for eng_name in self.ENGINES:
                evs = self.events[eng_name]
                if not evs:
                    continue

                def mk(evs=evs):
                    def body(eng):
                        for wl, fn, inc, amt in evs:
                            for s, v in wl:
                                eng.wait_ge(s.h, v)
                            instr = fn(eng)
                            if inc is not None:
                                instr.then_inc(inc.h, amt)
                    return body

                getattr(block, eng_name)(mk())


def host_weights(Wq1, Wk1, Wq2, Wk2, Wv, Wo, bo, wg, wu, wd):
    d = {}
    Wqk = np.concatenate([np.asarray(Wq1), np.asarray(Wk1)], axis=0)
    d["wqkT"] = np.ascontiguousarray(Wqk.reshape(128, 8, 128).transpose(1, 2, 0)).astype(np.float32)
    d["wq2T"] = np.ascontiguousarray(np.asarray(Wq2).reshape(128, 8, 128).transpose(1, 2, 0)).astype(np.float32)
    d["wk2T"] = np.ascontiguousarray(np.asarray(Wk2).reshape(128, 8, 128).transpose(1, 2, 0)).astype(np.float32)
    d["wvT"] = np.ascontiguousarray(np.asarray(Wv).reshape(1024, 8, 128).transpose(1, 2, 0)).astype(np.float32)
    woT = np.asarray(Wo).reshape(8, 128, 8, 128).transpose(3, 2, 0, 1)
    d["woT"] = np.ascontiguousarray(woT.reshape(128, 8192)).astype(np.float32)
    d["bo_t"] = np.ascontiguousarray(np.asarray(bo).reshape(8, 128).T).astype(np.float32)
    wgt = np.asarray(wg).reshape(24, 128, 8, 128).transpose(0, 3, 2, 1)
    d["wgT"] = np.ascontiguousarray(wgt.reshape(24, 128, 1024)).astype(np.int32)
    wut = np.asarray(wu).reshape(24, 128, 8, 128).transpose(0, 3, 2, 1)
    d["wuT"] = np.ascontiguousarray(wut.reshape(24, 128, 1024)).astype(np.int32)
    wdt = np.asarray(wd).reshape(8, 128, 24, 128).transpose(2, 3, 0, 1)
    d["wdT"] = np.ascontiguousarray(wdt.reshape(24, 128, 1024)).astype(np.int32)
    Eq1 = np.zeros((128, 1024), np.float32)
    Ek1 = np.zeros((128, 1024), np.float32)
    E2 = np.zeros((128, 1024), np.float32)
    for cc in range(8):
        for hh in range(2):
            h = 2 * cc + hh
            for r in range(R1):
                for s in range(R2):
                    m = cc * 128 + hh * 64 + r * 8 + s
                    Eq1[h * 4 + r, m] = 1.0
                    Ek1[64 + h * 4 + r, m] = 1.0
                    E2[h * 8 + s, m] = 1.0
    d["Eq1"] = Eq1.astype(BF)
    d["Ek1"] = Ek1.astype(BF)
    d["E2"] = E2.astype(BF)
    d["ones_row2048"] = np.ones((1, 2048), BF)
    d["onescol_bf"] = np.ones((128, 16), BF)
    d["onesrow_f32"] = np.ones((1, 128), np.float32)
    d["onesrow_scaled"] = np.full((1, 128), QSCL, np.float32)
    op64 = np.zeros((65, 64), np.float32)
    op64[64, :] = 1.0
    d["ones_p64"] = op64
    d["epscell"] = np.full((1, 8), EPS, np.float32)
    tm = np.where(np.arange(128)[None, :] <= np.arange(128)[:, None], 0.0, -1e9)
    d["trimask"] = tm.astype(BF)
    d["ident"] = np.eye(128, dtype=np.float32).astype(BF)
    return d


HOST_SPECS = [
    ("wqkT", [8, 128, 128], F32), ("wq2T", [8, 128, 128], F32), ("wk2T", [8, 128, 128], F32),
    ("wvT", [8, 128, 1024], F32), ("woT", [128, 8192], F32), ("bo_t", [128, 8], F32),
    ("wgT", [24, 128, 1024], I32), ("wuT", [24, 128, 1024], I32), ("wdT", [24, 128, 1024], I32),
    ("Eq1", [128, 1024], BF16), ("Ek1", [128, 1024], BF16), ("E2", [128, 1024], BF16),
    ("ones_row2048", [1, 2048], BF16), ("onescol_bf", [128, 16], BF16),
    ("onesrow_f32", [1, 128], F32), ("onesrow_scaled", [1, 128], F32),
    ("ones_p64", [65, 64], F32), ("epscell", [1, 8], F32),
    ("trimask", [128, 128], BF16), ("ident", [128, 128], BF16),
]

TAP_SPECS = {
    "xn": ([128, 16384], BF16),
    "q1k1": ([128, 2048], BF16),
    "q2": ([128, 2048], BF16),
    "k2": ([128, 2048], BF16),
    "Q33": ([128, 16384], BF16),
    "K33": ([128, 16384], BF16),
    "Vaug": ([128, 16640], BF16),
    "attn_fm": ([128, 16384], BF16),
    "resid2": ([128, 16384], F32),
    "xn2": ([128, 16384], BF16),
    "dbg": ([128, 2048], F32),
}


def build(taps=(), n_heads=H, phases=("ln1", "proj", "attn", "wo", "ln2", "ffn")):
    nc = bass.Bass("TRN2", target_bir_lowering=False)
    xT = nc.dram_tensor("xT", [1024, 2048], F32, kind="ExternalInput").ap()
    din = {}
    for name, shape, dt in HOST_SPECS:
        din[name] = nc.dram_tensor(name, shape, dt, kind="ExternalInput").ap()
    out = nc.dram_tensor("out", [1024, 2048], F32, kind="ExternalOutput").ap()
    dtap = {t: nc.dram_tensor("tap_" + t, *TAP_SPECS[t], kind="ExternalOutput").ap() for t in taps}

    BASE = 16896

    def sb(name, shape, dt, off):
        return nc.alloc_sbuf_tensor_at(name, shape, dt, offset=BASE + off).ap()

    A = 0
    xf32 = sb("xf32", [128, 16384], F32, A)
    Q33 = [sb(f"Q33_{c}", [128, 2048], BF16, A + c * 4096) for c in range(8)]
    K33 = [sb(f"K33_{c}", [128, 2048], BF16, A + 32768 + c * 4096) for c in range(8)]
    resid = [sb(f"resid_{c}", [128, 2048], F32, A + c * 8192) for c in range(8)]
    B = 65536
    xbf = sb("xbf", [128, 16384], BF16, B)
    xn = sb("xn", [128, 16384], BF16, B)
    attn_fm = [sb(f"attnfm_{c}", [128, 2048], BF16, B + c * 4096) for c in range(8)]
    xn2 = sb("xn2", [128, 16384], BF16, B)
    CC = 98304
    V_aug = sb("V_aug", [128, 16640], BF16, CC)
    wgu = [sb(f"wgu_{i}", [128, 1024], BF16, CC + i * 2048) for i in range(4)]
    wd_oct = [sb(f"wdoct_{j}", [128, 1024], BF16, CC + 8192 + j * 2048) for j in range(8)]
    g_s = sb("g_s", [128, 2048], BF16, CC + 24576)
    h_oct = [sb(f"hoct_{j}", [128, 2048], BF16, CC + 28672 + j * 4096) for j in range(8)]
    D = 147456
    q1k1_sb = sb("q1k1_sb", [128, 2048], BF16, D)
    q2_sb = sb("q2_sb", [128, 2048], BF16, D + 4096)
    k2_sb = sb("k2_sb", [128, 2048], BF16, D + 8192)
    exsc = sb("exsc", [128, 2048], BF16, D + 12288)
    wvT_sb = sb("wvT_sb", [128, 8192], BF16, D + 16384)
    Pt = [sb(f"Pt_{i}", [128, 512], BF16, D + i * 1024) for i in range(3)]
    Mpos = sb("Mpos", [128, 16], F32, D + 3072)
    Mbf = sb("Mbf", [128, 128], BF16, D + 3200)
    Mt = sb("Mt", [128, 128], BF16, D + 3456)
    rb_sb = sb("rb_sb", [64, 512], F32, D + 3712)
    trimask_sb = sb("trimask_sb", [128, 128], BF16, D + 5760)
    ident_sb = sb("ident_sb", [128, 128], BF16, D + 6016)
    acc_sb = sb("acc_sb", [65, 2048], F32, D + 6272)
    xres = [sb(f"xres_{i}", [128, 2048], F32, D + i * 8192) for i in range(2)]
    woT_sb = sb("woT_sb", [128, 8192], BF16, D + 16384)
    tscratch = sb("tscratch", [128, 2048], F32, 180224)
    tscratch2 = sb("tscratch2", [128, 2048], F32, 151552)
    wqkT_sb = sb("wqkT_sb", [128, 1024], BF16, 180224)
    wq2T_sb = sb("wq2T_sb", [128, 1024], BF16, 182272)
    wk2T_sb = sb("wk2T_sb", [128, 1024], BF16, 184320)
    rows = sb("rows", [65, 4096], F32, 188416)
    CO = 204800
    onescol_bf = sb("c_onescol", [128, 16], BF16, CO)
    onesrow_f32 = sb("c_onesrow", [1, 128], F32, CO + 32)
    onesrow_scaled = sb("c_onesrow_s", [1, 128], F32, CO + 544)
    ones_p64 = sb("c_ones64", [65, 64], F32, CO + 1056)
    epscell = sb("c_eps", [1, 8], F32, CO + 1312)
    Eq1_sb = sb("c_Eq1", [128, 1024], BF16, CO + 1344)
    Ek1_sb = sb("c_Ek1", [128, 1024], BF16, CO + 3392)
    E2_sb = sb("c_E2", [128, 1024], BF16, CO + 5440)
    bo_sb = sb("bo_sb", [128, 8], F32, CO + 7488)

    pbig = nc.alloc_psum_tensor("pbig", [128, 2048], F32).ap()
    pbig2 = nc.alloc_psum_tensor("pbig2", [128, 2048], F32).ap()

    S = Sched(nc)
    ld_s = S.sem("ld_s")
    ld_g = S.sem("ld_g")
    ld_c = S.sem("ld_c")
    s_pe = S.sem("s_pe")
    s_dve = S.sem("s_dve")
    s_act = S.sem("s_act")
    s_pool = S.sem("s_pool")
    st = S.sem("st")

    def W(*pairs):
        return list(pairs)

    S.op("sync", lambda e: e.dma_start(out=xf32.rearrange("p (a t) -> p a t", a=8),
                                       in_=xT.rearrange("(a p) t -> p a t", p=128)), inc=ld_s, dma=True)
    x_loaded = ld_s.v
    for dst, src in [(onescol_bf, din["onescol_bf"]), (onesrow_f32, din["onesrow_f32"]),
                     (onesrow_scaled, din["onesrow_scaled"]), (ones_p64, din["ones_p64"]),
                     (epscell, din["epscell"]), (Eq1_sb, din["Eq1"]), (Ek1_sb, din["Ek1"]),
                     (E2_sb, din["E2"])]:
        S.op("sync", lambda e, d=dst, s=src: e.dma_start(out=d, in_=s), inc=ld_s, dma=True)
    consts_loaded = ld_s.v
    for dst, src in [(wvT_sb.rearrange("p (a j) -> p a j", a=8), din["wvT"].rearrange("a p j -> p a j")),
                     (bo_sb, din["bo_t"])]:
        S.op("gpsimd", lambda e, d=dst, s=src: e.dma_start(out=d, in_=s), inc=ld_g, dma=True)
    attw_loaded = ld_g.v

    def emit_ln(src_chunks, stat_bf, dst_bf, rstd_row_lhs, first_wait):
        sq_mark = {0: 0, 1: 0}
        sqbs = [exsc, q1k1_sb]
        cast_marks = []
        for cc in range(8):
            xb_cc = stat_bf[:, cc * 2048:(cc + 1) * 2048]
            S.op("vector", lambda e, s=src_chunks[cc], d=xb_cc: e.tensor_copy(d, s),
                 waits=W(first_wait), inc=s_dve)
            cast_done = s_dve.v
            cast_marks.append(cast_done)
            sqb = sqbs[cc % 2]
            S.op("scalar", lambda e, s=xb_cc, d=sqb: e.activation(d, s, AF.Square),
                 waits=W((s_dve, cast_done), (s_pe, sq_mark[cc % 2])), inc=s_act)
            sq_done = s_act.v
            for j in range(4):
                c0, c1 = j * 512, (j + 1) * 512
                S.op("tensor", lambda e, s=xb_cc, c0=c0, c1=c1, f=(cc == 0), l=(cc == 7): e.matmul(
                    pbig[0:1, c0:c1], onescol_bf[:, 0:1], s[:, c0:c1], start=f, stop=l),
                    waits=W((s_dve, cast_done), (ld_s, consts_loaded)), inc=s_pe)
                S.op("tensor", lambda e, sqb=sqb, c0=c0, c1=c1, f=(cc == 0), l=(cc == 7): e.matmul(
                    pbig2[0:1, c0:c1], onescol_bf[:, 0:1], sqb[:, c0:c1], start=f, stop=l),
                    waits=W((s_act, sq_done)), inc=s_pe)
            sq_mark[cc % 2] = s_pe.v
        stats_done = s_pe.v
        nm = rows[0:1, 0:2048]
        rstd = rows[0:1, 2048:4096]
        e2 = rows[32:33, 0:2048]
        ms = rows[32:33, 2048:4096]
        var = rows[64:65, 0:2048]
        sqv = rows[64:65, 2048:4096]
        S.op("scalar", lambda e: e.activation(nm, pbig[0:1, :], AF.Copy, scale=-1.0 / 1024),
             waits=W((s_pe, stats_done)), inc=s_act)
        S.op("scalar", lambda e: e.activation(e2, pbig2[0:1, :], AF.Copy, scale=1.0 / 1024), inc=s_act)
        nm_done = s_act.v
        S.op("scalar", lambda e: e.activation(ms, nm, AF.Square), waits=W((s_act, nm_done)), inc=s_act)
        ms_done = s_act.v
        S.op("vector", lambda e: e.tensor_sub(var, e2, ms), waits=W((s_act, ms_done)), inc=s_dve)
        var_done = s_dve.v
        S.op("scalar", lambda e: e.activation(sqv, var, AF.Sqrt, bias=epscell[0:1, 0:1]),
             waits=W((s_dve, var_done), (ld_s, consts_loaded)), inc=s_act)
        sqv_done = s_act.v
        S.op("vector", lambda e: e.reciprocal(rstd, sqv), waits=W((s_act, sqv_done)), inc=s_dve)
        rstd_done = s_dve.v
        for j in range(4):
            c0, c1 = j * 512, (j + 1) * 512
            S.op("tensor", lambda e, c0=c0, c1=c1: e.matmul(
                pbig[:, c0:c1], onesrow_f32, nm[:, c0:c1], start=True, stop=True),
                waits=W((s_act, ms_done)), inc=s_pe)
            S.op("tensor", lambda e, c0=c0, c1=c1: e.matmul(
                pbig2[:, c0:c1], rstd_row_lhs, rstd[:, c0:c1], start=True, stop=True),
                waits=W((s_dve, rstd_done)), inc=s_pe)
        bc_done = s_pe.v
        mul_done = [0, 0]
        for cc in range(8):
            ts = tscratch if cc % 2 == 0 else tscratch2
            S.op("vector", lambda e, s=src_chunks[cc], ts=ts: e.tensor_add(ts, s, pbig),
                 waits=W((s_pe, bc_done), (s_dve, mul_done[cc % 2])), inc=s_dve)
            t_done = s_dve.v
            S.op("vector", lambda e, d=dst_bf[:, cc * 2048:(cc + 1) * 2048], ts=ts:
                 e.tensor_mul(d, ts, pbig2), waits=W((s_dve, t_done)), inc=s_dve)
            mul_done[cc % 2] = s_dve.v
        return s_dve.v

    xf32_chunks = [xf32[:, cc * 2048:(cc + 1) * 2048] for cc in range(8)]
    ln1_done = 0
    if "ln1" in phases:
        ln1_done = emit_ln(xf32_chunks, xbf, xn, onesrow_f32, (ld_s, x_loaded))
        if "xn" in taps:
            S.op("gpsimd", lambda e: e.dma_start(out=dtap["xn"], in_=xn),
                 waits=W((s_dve, ln1_done)), inc=ld_c, dma=True)
    for dst, src in [(wqkT_sb.rearrange("p (a j) -> p a j", a=8), din["wqkT"].rearrange("a p j -> p a j")),
                     (wq2T_sb.rearrange("p (a j) -> p a j", a=8), din["wq2T"].rearrange("a p j -> p a j")),
                     (wk2T_sb.rearrange("p (a j) -> p a j", a=8), din["wk2T"].rearrange("a p j -> p a j"))]:
        S.op("gpsimd", lambda e, d=dst, s=src: e.dma_start(out=d, in_=s),
             waits=W((s_dve, ln1_done)), inc=ld_g, dma=True)
    attw_loaded = ld_g.v

    proj_act_done = s_act.v
    expans_done = s_dve.v
    k33ones_done = 0
    if "proj" in phases:
        def proj128(wsb, dst_sb, extra_waits, psum):
            for cc in range(8):
                for j in range(4):
                    c0, c1 = j * 512, (j + 1) * 512
                    S.op("tensor", lambda e, cc=cc, c0=c0, c1=c1, w=wsb, p=psum,
                         f=(cc == 0), l=(cc == 7): e.matmul(
                        p[:, c0:c1], w[:, cc * 128:(cc + 1) * 128],
                        xn[:, cc * 2048 + c0: cc * 2048 + c1], start=f, stop=l),
                        waits=W(*extra_waits) if cc == 0 and j == 0 else (), inc=s_pe)
            m = s_pe.v
            S.op("vector", lambda e, d=dst_sb, p=psum: e.tensor_copy(d, p),
                 waits=W((s_pe, m)), inc=s_dve)
            return s_dve.v

        ev1 = proj128(wqkT_sb, q1k1_sb, [(s_dve, ln1_done), (ld_g, attw_loaded)], pbig)
        ev2 = proj128(wq2T_sb, q2_sb, [(s_dve, ln1_done)], pbig2)
        ev3 = proj128(wk2T_sb, k2_sb, [(s_dve, ev1)], pbig)
        if "q1k1" in taps:
            S.op("gpsimd", lambda e: e.dma_start(out=dtap["q1k1"], in_=q1k1_sb),
                 waits=W((s_dve, ev1)), inc=ld_c, dma=True)
        if "q2" in taps:
            S.op("gpsimd", lambda e: e.dma_start(out=dtap["q2"], in_=q2_sb),
                 waits=W((s_dve, ev2)), inc=ld_c, dma=True)
        if "k2" in taps:
            S.op("gpsimd", lambda e: e.dma_start(out=dtap["k2"], in_=k2_sb),
                 waits=W((s_dve, ev3)), inc=ld_c, dma=True)

        exp_free = {0: 0}
        vhalf_guard = {0: 0, 1: 0}
        S.op("vector", lambda e: e.memset(
            V_aug.rearrange("p (x f) -> p x f", f=65)[:, :, 64:65], 1.0), inc=s_dve)
        vones = s_dve.v
        for c in range(8):
            for j in range(4):
                c0, c1 = j * 512, (j + 1) * 512
                S.op("tensor", lambda e, c=c, c0=c0, c1=c1: e.matmul(
                    pbig2[:, c0:c1], Eq1_sb[0:64, c * 128:(c + 1) * 128], q1k1_sb[0:64, c0:c1],
                    start=True, stop=True),
                    waits=W((s_dve, ev2), (s_dve, ev1), (s_act, exp_free[0]),
                            (ld_s, consts_loaded)) if j == 0 else (),
                    inc=s_pe)
            mq1 = s_pe.v
            S.op("scalar", lambda e: e.activation(exsc, pbig2, AF.Copy),
                 waits=W((s_pe, mq1)), inc=s_act)
            evq = s_act.v
            for j in range(4):
                c0, c1 = j * 512, (j + 1) * 512
                S.op("tensor", lambda e, c=c, c0=c0, c1=c1: e.matmul(
                    pbig2[:, c0:c1], E2_sb[:, c * 128:(c + 1) * 128], q2_sb[:, c0:c1],
                    start=True, stop=True),
                    waits=W((s_act, evq), (s_dve, ev2)) if j == 0 else (), inc=s_pe)
            mq2 = s_pe.v
            S.op("vector", lambda e, c=c: e.tensor_mul(Q33[c], exsc, pbig2),
                 waits=W((s_pe, mq2), (s_act, evq)), inc=s_dve)
            mulq = s_dve.v
            for j in range(4):
                c0, c1 = j * 512, (j + 1) * 512
                S.op("tensor", lambda e, c=c, c0=c0, c1=c1: e.matmul(
                    pbig2[:, c0:c1], Ek1_sb[64:128, c * 128:(c + 1) * 128], q1k1_sb[64:128, c0:c1],
                    start=True, stop=True),
                    waits=W((s_dve, mulq), (s_dve, ev1)) if j == 0 else (), inc=s_pe)
            mk1 = s_pe.v
            S.op("scalar", lambda e: e.activation(exsc, pbig2, AF.Copy),
                 waits=W((s_pe, mk1), (s_dve, mulq)), inc=s_act)
            evk = s_act.v
            for j in range(4):
                c0, c1 = j * 512, (j + 1) * 512
                S.op("tensor", lambda e, c=c, c0=c0, c1=c1: e.matmul(
                    pbig2[:, c0:c1], E2_sb[:, c * 128:(c + 1) * 128], k2_sb[:, c0:c1],
                    start=True, stop=True),
                    waits=W((s_act, evk), (s_dve, ev3)) if j == 0 else (), inc=s_pe)
            mk2 = s_pe.v
            S.op("vector", lambda e, c=c: e.tensor_mul(K33[c], exsc, pbig2),
                 waits=W((s_pe, mk2), (s_act, evk)), inc=s_dve)
            exp_free[0] = s_act.v
            ev2b = s_dve.v
            for ti in (2 * c, 2 * c + 1):
                half = ti % 2
                for cc in range(8):
                    for j in range(2):
                        c0, c1 = half * 1024 + j * 512, half * 1024 + (j + 1) * 512
                        S.op("tensor", lambda e, ti=ti, cc=cc, c0=c0, c1=c1,
                             f=(cc == 0), l=(cc == 7), half=half: e.matmul(
                            pbig[:, c0:c1], xn[:, cc * 2048 + ti * 128: cc * 2048 + (ti + 1) * 128],
                            wvT_sb[:, cc * 1024 + (c0 - half * 1024): cc * 1024 + (c1 - half * 1024)],
                            start=f, stop=l),
                            waits=W((s_act, vhalf_guard[half]), (s_dve, ln1_done),
                                    (ld_g, attw_loaded), (s_dve, vones)) if cc == 0 and j == 0 else (),
                            inc=s_pe)
                mv = s_pe.v
                S.op("scalar", lambda e, ti=ti, half=half: e.activation(
                    V_aug.rearrange("p (x f) -> p x f", f=65)[:, ti * 16:(ti + 1) * 16, 0:64],
                    pbig[:, half * 1024:(half + 1) * 1024].rearrange("p (h d) -> p h d", d=64), AF.Copy),
                    waits=W((s_pe, mv)), inc=s_act)
                vhalf_guard[half] = s_act.v
        expans_done = s_dve.v
        for c in range(8):
            S.op("sync", lambda e, c=c: e.dma_start(out=K33[c][32:33, :], in_=din["ones_row2048"]),
                 waits=W((s_dve, expans_done)), inc=ld_s, dma=True)
            S.op("sync", lambda e, c=c: e.dma_start(out=K33[c][96:97, :], in_=din["ones_row2048"]),
                 inc=ld_s, dma=True)
        k33ones_done = ld_s.v

        proj_act_done = s_act.v
        S.op("gpsimd", lambda e: e.dma_start(out=woT_sb, in_=din["woT"]),
             waits=W((s_act, proj_act_done)), inc=ld_g, dma=True)
        woT_loaded = ld_g.v
        if "Q33" in taps:
            for c in range(8):
                S.op("gpsimd", lambda e, c=c: e.dma_start(
                    out=dtap["Q33"].rearrange("p (a t) -> p a t", a=8)[:, c:c + 1, :], in_=Q33[c]),
                    waits=W((s_dve, expans_done)), inc=ld_c, dma=True)
        if "K33" in taps:
            for c in range(8):
                S.op("gpsimd", lambda e, c=c: e.dma_start(
                    out=dtap["K33"].rearrange("p (a t) -> p a t", a=8)[:, c:c + 1, :], in_=K33[c]),
                    waits=W((s_dve, expans_done), (ld_s, k33ones_done)), inc=ld_c, dma=True)
        if "Vaug" in taps:
            S.op("gpsimd", lambda e: e.dma_start(out=dtap["Vaug"], in_=V_aug),
                 waits=W((s_act, proj_act_done)), inc=ld_c, dma=True)

    if "proj" not in phases:
        woT_loaded = 0
    attn_done_dve = s_dve.v
    if "attn" in phases:
        S.op("sync", lambda e: e.dma_start(out=trimask_sb, in_=din["trimask"]),
             waits=W((s_dve, expans_done)), inc=ld_s, dma=True)
        S.op("sync", lambda e: e.dma_start(out=ident_sb, in_=din["ident"]), inc=ld_s, dma=True)
        trimask_loaded = ld_s.v
        A_SLOTS = [pbig[:, 0:512], pbig[:, 512:1024], pbig2[:, 0:512]]
        S_SLOTS = [pbig[:, 1024:1536], pbig[:, 1536:2048], pbig2[:, 512:1024]]
        ACC_SLOTS = [pbig2[:, 1024:1536], pbig2[:, 1536:2048]]
        stA = {"red": [0, 0, 0], "w": 0}
        stB = {"exp": [0, 0, 0], "kb": 0}
        chain = {"last_transpose": 0, "last_gather": 0,
                 "acc_free": [0, 0],
                 "gather_mark": {}}

        def passA_gen(h):
            cc, base = h // 2, (h % 2) * 64
            for qt in range(16):
                keys = (qt + 1) * 128
                for rd in range((keys + 511) // 512):
                    k0 = rd * 512
                    w = min(keys - k0, 512)
                    ws = stA["w"] % 3
                    win = A_SLOTS[ws]
                    diag_here = k0 <= qt * 128 < k0 + w
                    S.op("tensor", lambda e, qt=qt, k0=k0, w=w, cc=cc, base=base, win=win,
                         l=(not diag_here): e.matmul(
                        win[:, 0:w],
                        Q33[cc][base:base + 32, qt * 128:(qt + 1) * 128],
                        K33[cc][base:base + 32, k0:k0 + w],
                        start=True, stop=l),
                        waits=W((s_dve, stA["red"][ws]), (s_act, proj_act_done)),
                        inc=s_pe)
                    if diag_here:
                        c0 = qt * 128 - k0
                        S.op("tensor", lambda e, c0=c0, win=win: e.matmul(
                            win[:, c0:c0 + 128], ident_sb, trimask_sb,
                            start=False, stop=True),
                            waits=W((ld_s, trimask_loaded)), inc=s_pe)
                    mm = s_pe.v
                    if rd == 0:
                        S.op("vector", lambda e, qt=qt, w=w, win=win: e.reduce_max(
                            Mpos[:, qt:qt + 1], win[:, 0:w], axis=AX),
                            waits=W((s_pe, mm)), inc=s_dve)
                    else:
                        S.op("vector", lambda e, w=w, win=win: e.reduce_max(
                            tscratch[:, 0:1], win[:, 0:w], axis=AX),
                            waits=W((s_pe, mm)), inc=s_dve)
                        rmk = s_dve.v
                        S.op("vector", lambda e, qt=qt: e.tensor_max(
                            Mpos[:, qt:qt + 1], Mpos[:, qt:qt + 1], tscratch[:, 0:1]),
                            waits=W((s_dve, rmk)), inc=s_dve)
                    stA["red"][ws] = s_dve.v
                    stA["w"] += 1
                    yield

        def passA_chain(h):
            cc, base = h // 2, (h % 2) * 64
            S.op("vector", lambda e: e.tensor_scalar_mul(Mbf[:, 0:16], Mpos, -1.0),
                 waits=W((s_dve, max(stA["red"])), (ld_c, chain["last_transpose"])), inc=s_dve)
            neg_done = s_dve.v
            S.op("sync", lambda e: e.dma_start(out=Mt, in_=Mbf, transpose=True),
                 waits=W((s_dve, neg_done), (ld_c, chain["last_gather"])), inc=ld_c, dma=True)
            chain["last_transpose"] = ld_c.v
            S.op("sync", lambda e, cc=cc, base=base: e.dma_start(
                out=Q33[cc][base + 32: base + 33, :], in_=Mt[0:16, :]),
                waits=W((ld_c, chain["last_transpose"])), inc=ld_c, dma=True)
            chain["last_gather"] = ld_c.v
            chain["gather_mark"][h] = ld_c.v

        def passB_gen(h):
            cc, base = h // 2, (h % 2) * 64
            gather_mark = chain["gather_mark"][h]
            for qc in range(4):
                nkc = 4 * (qc + 1)
                q0 = qc * 512
                aslot = qc % 2
                acc = ACC_SLOTS[aslot]
                pend = []
                for kc in range(nkc):
                    slot = stB["kb"] % 3
                    sreg = S_SLOTS[slot]
                    stB["kb"] += 1
                    S.op("tensor", lambda e, cc=cc, base=base, kc=kc, sreg=sreg, q0=q0: e.matmul(
                        sreg,
                        K33[cc][base:base + 33, kc * 128:(kc + 1) * 128],
                        Q33[cc][base:base + 33, q0:q0 + 512],
                        start=True, stop=True),
                        waits=W((ld_c, gather_mark), (ld_s, k33ones_done),
                                (s_act, max(stB["exp"][slot], proj_act_done))),
                        inc=s_pe)
                    smark = s_pe.v
                    S.op("scalar", lambda e, slot=slot, sreg=sreg: e.activation(
                        Pt[slot], sreg, AF.Exp, scale=SCALE),
                        waits=W((s_pe, smark)), inc=s_act)
                    stB["exp"][slot] = s_act.v
                    emark = s_act.v
                    pool_mark = 0
                    if kc >= 4 * qc:
                        w = (kc - 4 * qc + 1) * 128
                        bb = q0 - kc * 128
                        S.op("gpsimd", lambda e, b=slot, w=w, bb=bb: e.affine_select(
                            Pt[b][:, 0:w], Pt[b][:, 0:w], [[1, w]], ALU.is_ge, 0.0,
                            base=bb, channel_multiplier=-1),
                            waits=W((s_act, emark)), inc=s_pool)
                        pool_mark = s_pool.v
                    if len(pend) >= 2:
                        p = pend.pop(0)
                        S.op("tensor", p[0], waits=p[1], inc=s_pe)
                    fn = (lambda e, kc=kc, h=h, b=slot, acc=acc,
                          f=(kc == 0), l=(kc == nkc - 1): e.matmul(
                        acc[0:65, :],
                        V_aug[:, kc * 1040 + h * 65: kc * 1040 + (h + 1) * 65],
                        Pt[b], start=f, stop=l))
                    pend.append((fn, W((s_act, emark), (s_pool, pool_mark),
                                       (s_act, chain["acc_free"][aslot] if kc == 0 else 0))))
                    yield
                for p in pend:
                    S.op("tensor", p[0], waits=p[1], inc=s_pe)
                pv_done = s_pe.v
                S.op("scalar", lambda e, acc=acc, q0=q0: e.activation(
                    acc_sb[:, q0:q0 + 512], acc[0:65, :], AF.Copy),
                    waits=W((s_pe, pv_done)), inc=s_act)
                chain["acc_free"][aslot] = s_act.v
                ev = s_act.v
                rr = rows[64:65, q0:q0 + 512]
                S.op("vector", lambda e, rr=rr, q0=q0: e.reciprocal(rr, acc_sb[64:65, q0:q0 + 512]),
                     waits=W((s_act, ev)), inc=s_dve)
                rmark = s_dve.v
                S.op("tensor", lambda e, rr=rr: e.matmul(
                    pbig[0:64, 1024:1536], ones_p64[64:65, :], rr,
                    start=True, stop=True),
                    waits=W((s_dve, rmark), (s_act, stB["exp"][0])), inc=s_pe)
                bmark = s_pe.v
                S.op("scalar", lambda e: e.activation(rb_sb, pbig[0:64, 1024:1536], AF.Copy),
                     waits=W((s_pe, bmark), (s_dve, chain.get("mm", 0))), inc=s_act)
                stB["exp"][0] = s_act.v
                emark2 = s_act.v
                S.op("vector", lambda e, cc=cc, base=base, q0=q0: e.tensor_mul(
                    attn_fm[cc][base:base + 64, q0:q0 + 512], acc_sb[0:64, q0:q0 + 512], rb_sb),
                    waits=W((s_act, emark2)), inc=s_dve)
                chain["mm"] = s_dve.v

        for _ in passA_gen(0):
            pass
        passA_chain(0)
        for h in range(n_heads):
            gb = passB_gen(h)
            ga = passA_gen(h + 1) if h + 1 < n_heads else None
            while True:
                done_b = next(gb, "end") == "end"
                if ga is not None:
                    next(ga, "end")
                if done_b:
                    break
            if ga is not None:
                for _ in ga:
                    pass
                passA_chain(h + 1)
        attn_done_dve = s_dve.v
    pe_after_attn = s_pe.v
    if "attn" in phases:
        if "attn_fm" in taps:
            for c in range(8):
                S.op("gpsimd", lambda e, c=c: e.dma_start(
                    out=dtap["attn_fm"].rearrange("p (a t) -> p a t", a=8)[:, c:c + 1, :], in_=attn_fm[c]),
                    waits=W((s_dve, attn_done_dve)), inc=ld_c, dma=True)

    resid_done = attn_done_dve
    if "wo" in phases:
        xres_mark = [0, 0]
        consume_mark = [0, 0]
        wo_bank = [attn_done_dve, attn_done_dve]
        for m in range(2):
            S.op("sync", lambda e, m=m: e.dma_start(out=xres[m], in_=xT[m * 128:(m + 1) * 128, :]),
                 waits=W((s_dve, attn_done_dve)), inc=ld_s, dma=True)
            xres_mark[m] = ld_s.v
        for m in range(8):
            for qc in range(4):
                bank = (m * 4 + qc) % 2
                q0 = qc * 512
                for cc in range(8):
                    S.op("tensor", lambda e, m=m, cc=cc, q0=q0, bank=bank,
                         f=(cc == 0), l=(cc == 7): e.matmul(
                        pbig[:, bank * 512:(bank + 1) * 512],
                        woT_sb[:, (cc * 8 + m) * 128:(cc * 8 + m + 1) * 128],
                        attn_fm[cc][:, q0:q0 + 512], start=f, stop=l),
                        waits=W((s_dve, attn_done_dve), (s_dve, wo_bank[bank]),
                                (ld_g, woT_loaded)) if cc == 0 else (), inc=s_pe)
                wmark = s_pe.v
                if m == 0 and qc == 0 and "dbg" in taps:
                    S.op("vector", lambda e: e.tensor_copy(tscratch[:, 0:512], woT_sb[:, 0:512]),
                         waits=W((s_pe, wmark)), inc=s_dve)
                    S.op("vector", lambda e: e.tensor_copy(tscratch[:, 512:1024], attn_fm[0][:, 0:512]),
                         waits=W((ld_s, xres_mark[0])), inc=s_dve)
                    S.op("vector", lambda e: e.tensor_copy(tscratch[:, 1024:1536], woT_sb[:, 4096:4608]),
                         waits=W((ld_g, attw_loaded)), inc=s_dve)
                    dbg_mark = s_dve.v
                    S.op("gpsimd", lambda e: e.dma_start(out=dtap["dbg"], in_=tscratch),
                         waits=W((s_dve, dbg_mark)), inc=ld_c, dma=True)
                S.op("vector", lambda e, m=m, q0=q0, bank=bank: e.scalar_tensor_tensor(
                    resid[m][:, q0:q0 + 512], pbig[:, bank * 512:(bank + 1) * 512],
                    bo_sb[:, m:m + 1], xres[m % 2][:, q0:q0 + 512],
                    ALU.add, ALU.add),
                    waits=W((s_pe, wmark), (ld_s, xres_mark[m % 2])), inc=s_dve)
                wo_bank[bank] = s_dve.v
            consume_mark[m % 2] = s_dve.v
            if m + 2 < 8:
                S.op("sync", lambda e, m=m: e.dma_start(
                    out=xres[m % 2], in_=xT[(m + 2) * 128:(m + 3) * 128, :]),
                    waits=W((s_dve, consume_mark[m % 2])), inc=ld_s, dma=True)
                xres_mark[m % 2] = ld_s.v
        resid_done = s_dve.v
        if "resid2" in taps:
            for m in range(8):
                S.op("gpsimd", lambda e, m=m: e.dma_start(
                    out=dtap["resid2"].rearrange("p (a t) -> p a t", a=8)[:, m:m + 1, :], in_=resid[m]),
                    waits=W((s_dve, resid_done)), inc=ld_c, dma=True)

    ln2_done = resid_done
    if "ln2" in phases:
        ln2_done = emit_ln(resid, xbf, xn2, onesrow_scaled, (s_dve, resid_done))
        if "xn2" in taps:
            S.op("gpsimd", lambda e: e.dma_start(out=dtap["xn2"], in_=xn2),
                 waits=W((s_dve, ln2_done)), inc=ld_c, dma=True)

    ffn_done = ln2_done
    if "ffn" in phases:
        wgu_free = [pe_after_attn] * 4
        wd_free = pe_after_attn
        wload = {}
        gate_guard = 0
        up_guard = ln2_done
        down_bank = [resid_done, resid_done]
        h_consumed = 0

        def load_gu(hc):
            par = hc % 2
            S.op("gpsimd", lambda e, hc=hc, par=par: e.dma_start(
                out=wgu[0 + par], in_=din["wgT"][hc]),
                waits=W((s_pe, wgu_free[0 + par])), inc=ld_g, dma=True)
            S.op("gpsimd", lambda e, hc=hc, par=par: e.dma_start(
                out=wgu[2 + par], in_=din["wuT"][hc]),
                waits=W((s_pe, wgu_free[2 + par])), inc=ld_g, dma=True)
            wload[("gu", hc)] = ld_g.v

        def load_wd(hc):
            S.op("gpsimd", lambda e, hc=hc: e.dma_start(
                out=wd_oct[hc % 8], in_=din["wdT"][hc]),
                waits=W((s_pe, wd_free)), inc=ld_g, dma=True)
            wload[("wd", hc)] = ld_g.v

        load_gu(0)
        load_wd(0)
        for oct_ in range(3):
            for hj in range(8):
                hc = oct_ * 8 + hj
                par = hc % 2
                if hc + 1 < 24:
                    load_gu(hc + 1)
                if hj < 7 and hc + 1 < 24:
                    load_wd(hc + 1)
                for half in range(2):
                    for cc in range(8):
                        for j in range(2):
                            t0 = half * 1024 + j * 512
                            S.op("tensor", lambda e, cc=cc, t0=t0, j=j, par=par,
                                 f=(cc == 0), l=(cc == 7): e.matmul(
                                pbig[:, j * 512:(j + 1) * 512],
                                wgu[0 + par][:, cc * 128:(cc + 1) * 128],
                                xn2[:, cc * 2048 + t0: cc * 2048 + t0 + 512], start=f, stop=l),
                                waits=W((ld_g, wload[("gu", hc)]), (s_dve, ln2_done),
                                        (s_act, gate_guard)) if cc == 0 and j == 0 else (), inc=s_pe)
                    gmark = s_pe.v
                    wgu_free[0 + par] = gmark
                    S.op("scalar", lambda e, half=half: e.activation(
                        g_s[:, half * 1024:(half + 1) * 1024], pbig[:, 0:1024], AF.Silu, scale=127.0),
                        waits=W((s_pe, gmark)), inc=s_act)
                    gate_guard = s_act.v
                    for cc in range(8):
                        for j in range(2):
                            t0 = half * 1024 + j * 512
                            S.op("tensor", lambda e, cc=cc, t0=t0, j=j, par=par,
                                 f=(cc == 0), l=(cc == 7): e.matmul(
                                pbig[:, 1024 + j * 512: 1024 + (j + 1) * 512],
                                wgu[2 + par][:, cc * 128:(cc + 1) * 128],
                                xn2[:, cc * 2048 + t0: cc * 2048 + t0 + 512], start=f, stop=l),
                                waits=W((s_dve, up_guard)) if cc == 0 and j == 0 else (), inc=s_pe)
                    umark = s_pe.v
                    wgu_free[2 + par] = umark
                    S.op("vector", lambda e, hj=hj, half=half: e.tensor_mul(
                        h_oct[hj][:, half * 1024:(half + 1) * 1024],
                        g_s[:, half * 1024:(half + 1) * 1024], pbig[:, 1024:2048]),
                        waits=W((s_pe, umark), (s_act, gate_guard), (s_pe, h_consumed)), inc=s_dve)
                    up_guard = s_dve.v
            hoct_done = s_dve.v
            for m in range(8):
                for qc in range(4):
                    bank = (m * 4 + qc) % 2
                    q0 = qc * 512
                    for hj in range(8):
                        S.op("tensor", lambda e, hj=hj, m=m, q0=q0, bank=bank,
                             f=(hj == 0), l=(hj == 7): e.matmul(
                            pbig2[:, bank * 512:(bank + 1) * 512],
                            wd_oct[hj][:, m * 128:(m + 1) * 128], h_oct[hj][:, q0:q0 + 512],
                            start=f, stop=l),
                            waits=W((s_dve, hoct_done), (s_dve, down_bank[bank]),
                                    (ld_g, wload[("wd", oct_ * 8 + 7)])) if hj == 0 else (), inc=s_pe)
                    dmark = s_pe.v
                    S.op("vector", lambda e, m=m, q0=q0, bank=bank: e.tensor_add(
                        resid[m][:, q0:q0 + 512], resid[m][:, q0:q0 + 512],
                        pbig2[:, bank * 512:(bank + 1) * 512]),
                        waits=W((s_pe, dmark), (s_dve, resid_done)), inc=s_dve)
                    down_bank[bank] = s_dve.v
            wd_free = s_pe.v
            h_consumed = s_pe.v
            if oct_ + 1 < 3:
                load_wd((oct_ + 1) * 8)
        ffn_done = s_dve.v

    S.op("sync", lambda e: e.dma_start(
        out=out.rearrange("(a p) t -> p a t", p=128),
        in_=xf32.rearrange("p (a t) -> p a t", a=8)),
        waits=W((s_dve, ffn_done)), inc=st, dma=True)
    S.op("gpsimd", lambda e: e.engine_nop(),
         waits=W((st, st.v), (ld_c, ld_c.v)))

    S.emit()
    return nc


def kernel(L0_attn_x, L0_attn_Wq1, L0_attn_Wk1, L0_attn_Wq2, L0_attn_Wk2,
           L0_attn_Wv, L0_attn_Wo, L0_attn_bo, L0_ffn_w_gate, L0_ffn_w_up,
           L0_ffn_w_down):
    from concourse.bass_utils import run_bass_kernel_spmd
    x = np.ascontiguousarray(np.asarray(L0_attn_x, np.float32))
    hw = host_weights(L0_attn_Wq1, L0_attn_Wk1, L0_attn_Wq2, L0_attn_Wk2,
                      L0_attn_Wv, L0_attn_Wo, L0_attn_bo,
                      L0_ffn_w_gate, L0_ffn_w_up, L0_ffn_w_down)
    nc = build()
    in_maps = []
    for b in range(8):
        m = {"xT": np.ascontiguousarray(x[b].T)}
        m.update(hw)
        in_maps.append(m)
    res = run_bass_kernel_spmd(nc, in_maps, core_ids=list(range(8)))
    out = np.stack([np.asarray(res.results[b]["out"], np.float32).T for b in range(8)], axis=0)
    return np.ascontiguousarray(out)


# version 13
# speedup vs baseline: 1.2327x; 1.2327x over previous
import numpy as np
import ml_dtypes
import concourse.bass as bass
import concourse.mybir as mybir

F32 = mybir.dt.float32
BF16 = mybir.dt.bfloat16
I32 = mybir.dt.int32
AF = mybir.ActivationFunctionType
ALU = mybir.AluOpType
AX = mybir.AxisListType.X

H = 16
R1, R2 = 4, 8
SCALE = float((R1 * R2) ** -0.5)
QSCL = float(1.0 / (127.0 * 127.0))
EPS = 1e-5
BF = ml_dtypes.bfloat16


class Sem:
    __slots__ = ("h", "name", "v", "incers")

    def __init__(self, handle, name):
        self.h = handle
        self.name = name
        self.v = 0
        self.incers = set()


class Sched:
    ENGINES = ("sync", "scalar", "vector", "tensor", "gpsimd")

    def __init__(self, nc):
        self.nc = nc
        self.events = {e: [] for e in self.ENGINES}
        self.last_wait = {e: {} for e in self.ENGINES}

    def sem(self, name):
        return Sem(self.nc.alloc_semaphore(name=name), name)

    def op(self, engine, fn, waits=(), inc=None, dma=False):
        amt = 16 if dma else 1
        wl = []
        best = {}
        for s, v in waits:
            if v <= 0:
                continue
            if v > best.get(s.name, (None, -1))[1]:
                best[s.name] = (s, v)
        for s, v in best.values():
            if v <= self.last_wait[engine].get(s.name, -1):
                continue
            self.last_wait[engine][s.name] = v
            wl.append((s, v))
        if inc is not None:
            inc.v += amt
            inc.incers.add(engine)
        self.events[engine].append((wl, fn, inc, amt))
        return inc.v if inc is not None else None

    def emit(self):
        with self.nc.Block() as block:
            for eng_name in self.ENGINES:
                evs = self.events[eng_name]
                if not evs:
                    continue

                def mk(evs=evs):
                    def body(eng):
                        for wl, fn, inc, amt in evs:
                            for s, v in wl:
                                eng.wait_ge(s.h, v)
                            instr = fn(eng)
                            if inc is not None:
                                instr.then_inc(inc.h, amt)
                    return body

                getattr(block, eng_name)(mk())


def host_weights(Wq1, Wk1, Wq2, Wk2, Wv, Wo, bo, wg, wu, wd):
    d = {}
    Wqk = np.concatenate([np.asarray(Wq1), np.asarray(Wk1)], axis=0)
    d["wqkT"] = np.ascontiguousarray(Wqk.reshape(128, 8, 128).transpose(1, 2, 0)).astype(np.float32)
    d["wq2T"] = np.ascontiguousarray(np.asarray(Wq2).reshape(128, 8, 128).transpose(1, 2, 0)).astype(np.float32)
    d["wk2T"] = np.ascontiguousarray(np.asarray(Wk2).reshape(128, 8, 128).transpose(1, 2, 0)).astype(np.float32)
    d["wvT"] = np.ascontiguousarray(np.asarray(Wv).reshape(1024, 8, 128).transpose(1, 2, 0)).astype(np.float32)
    woT = np.asarray(Wo).reshape(8, 128, 8, 128).transpose(3, 2, 0, 1)
    d["woT"] = np.ascontiguousarray(woT.reshape(128, 8192)).astype(np.float32)
    d["bo_t"] = np.ascontiguousarray(np.asarray(bo).reshape(8, 128).T).astype(np.float32)
    wgt = np.asarray(wg).reshape(24, 128, 8, 128).transpose(0, 3, 2, 1)
    d["wgT"] = np.ascontiguousarray(wgt.reshape(24, 128, 1024)).astype(np.int32)
    wut = np.asarray(wu).reshape(24, 128, 8, 128).transpose(0, 3, 2, 1)
    d["wuT"] = np.ascontiguousarray(wut.reshape(24, 128, 1024)).astype(np.int32)
    wdt = np.asarray(wd).reshape(8, 128, 24, 128).transpose(2, 3, 0, 1)
    d["wdT"] = np.ascontiguousarray(wdt.reshape(24, 128, 1024)).astype(np.int32)
    Eq1 = np.zeros((128, 1024), np.float32)
    Ek1 = np.zeros((128, 1024), np.float32)
    E2 = np.zeros((128, 1024), np.float32)
    for cc in range(8):
        for hh in range(2):
            h = 2 * cc + hh
            for r in range(R1):
                for s in range(R2):
                    m = cc * 128 + hh * 64 + r * 8 + s
                    Eq1[h * 4 + r, m] = 1.0
                    Ek1[64 + h * 4 + r, m] = 1.0
                    E2[h * 8 + s, m] = 1.0
    d["Eq1"] = Eq1.astype(BF)
    d["Ek1"] = Ek1.astype(BF)
    d["E2"] = E2.astype(BF)
    d["ones_row2048"] = np.ones((1, 2048), BF)
    d["onescol_bf"] = np.ones((128, 16), BF)
    d["onesrow_f32"] = np.ones((1, 128), np.float32)
    d["onesrow_scaled"] = np.full((1, 128), QSCL, np.float32)
    op64 = np.zeros((65, 64), np.float32)
    op64[64, :] = 1.0
    d["ones_p64"] = op64
    d["ones_p64b"] = op64.astype(BF)
    d["epscell"] = np.full((1, 8), EPS, np.float32)
    tm = np.where(np.arange(128)[None, :] <= np.arange(128)[:, None], 0.0, -1e9)
    d["trimask"] = tm.astype(BF)
    d["ident"] = np.eye(128, dtype=np.float32).astype(BF)
    return d


HOST_SPECS = [
    ("wqkT", [8, 128, 128], F32), ("wq2T", [8, 128, 128], F32), ("wk2T", [8, 128, 128], F32),
    ("wvT", [8, 128, 1024], F32), ("woT", [128, 8192], F32), ("bo_t", [128, 8], F32),
    ("wgT", [24, 128, 1024], I32), ("wuT", [24, 128, 1024], I32), ("wdT", [24, 128, 1024], I32),
    ("Eq1", [128, 1024], BF16), ("Ek1", [128, 1024], BF16), ("E2", [128, 1024], BF16),
    ("ones_row2048", [1, 2048], BF16), ("onescol_bf", [128, 16], BF16),
    ("onesrow_f32", [1, 128], F32), ("onesrow_scaled", [1, 128], F32),
    ("ones_p64", [65, 64], F32), ("ones_p64b", [65, 64], BF16), ("epscell", [1, 8], F32),
    ("trimask", [128, 128], BF16), ("ident", [128, 128], BF16),
]

TAP_SPECS = {
    "xn": ([128, 16384], BF16),
    "q1k1": ([128, 2048], BF16),
    "q2": ([128, 2048], BF16),
    "k2": ([128, 2048], BF16),
    "Q33": ([128, 16384], BF16),
    "K33": ([128, 16384], BF16),
    "Vaug": ([128, 16640], BF16),
    "attn_fm": ([128, 16384], BF16),
    "resid2": ([128, 16384], F32),
    "xn2": ([128, 16384], BF16),
    "dbg": ([128, 2048], F32),
}


def build(taps=(), n_heads=H, phases=("ln1", "proj", "attn", "wo", "ln2", "ffn")):
    nc = bass.Bass("TRN2", target_bir_lowering=False)
    xT = nc.dram_tensor("xT", [1024, 2048], F32, kind="ExternalInput").ap()
    din = {}
    for name, shape, dt in HOST_SPECS:
        din[name] = nc.dram_tensor(name, shape, dt, kind="ExternalInput").ap()
    out = nc.dram_tensor("out", [1024, 2048], F32, kind="ExternalOutput").ap()
    dtap = {t: nc.dram_tensor("tap_" + t, *TAP_SPECS[t], kind="ExternalOutput").ap() for t in taps}

    BASE = 16896

    def sb(name, shape, dt, off):
        return nc.alloc_sbuf_tensor_at(name, shape, dt, offset=BASE + off).ap()

    A = 0
    xf32 = sb("xf32", [128, 16384], F32, A)
    Q33 = [sb(f"Q33_{c}", [128, 2048], BF16, A + c * 4096) for c in range(8)]
    K33 = [sb(f"K33_{c}", [128, 2048], BF16, A + 32768 + c * 4096) for c in range(8)]
    resid = [sb(f"resid_{c}", [128, 2048], F32, A + c * 8192) for c in range(8)]
    B = 65536
    xbf = sb("xbf", [128, 16384], BF16, B)
    xn = sb("xn", [128, 16384], BF16, B)
    attn_fm = [sb(f"attnfm_{c}", [128, 2048], BF16, B + c * 4096) for c in range(8)]
    xn2 = sb("xn2", [128, 16384], BF16, B)
    CC = 98304
    V_aug = sb("V_aug", [128, 16640], BF16, CC)
    wgu = [sb(f"wgu_{i}", [128, 1024], BF16, CC + i * 2048) for i in range(4)]
    wd_oct = [sb(f"wdoct_{j}", [128, 1024], BF16, CC + 8192 + j * 2048) for j in range(8)]
    g_s = sb("g_s", [128, 2048], BF16, CC + 24576)
    h_oct = [sb(f"hoct_{j}", [128, 2048], BF16, CC + 28672 + j * 4096) for j in range(8)]
    ln2stat = sb("ln2stat", [128, 16384], BF16, CC + 28672)
    D = 147456
    q1k1_sb = sb("q1k1_sb", [128, 2048], BF16, D)
    q2_sb = sb("q2_sb", [128, 2048], BF16, D + 4096)
    k2_sb = sb("k2_sb", [128, 2048], BF16, D + 8192)
    exsc = sb("exsc", [128, 2048], BF16, D + 12288)
    wvT_sb = sb("wvT_sb", [128, 8192], BF16, D + 16384)
    Pt = [sb(f"Pt_{i}", [128, 1024], BF16, D + i * 2048) for i in range(2)]
    Mpos = sb("Mpos", [128, 32], F32, D + 4096)
    Mbf = sb("Mbf", [128, 128], BF16, D + 4224)
    Mt = sb("Mt", [128, 128], BF16, D + 4480)
    rb_sb = sb("rb_sb", [64, 512], F32, D + 4736)
    trimask_sb = sb("trimask_sb", [128, 128], BF16, D + 6784)
    ident_sb = sb("ident_sb", [128, 128], BF16, D + 7040)
    rrb_sb = sb("rrb_sb", [65, 2048], BF16, 184320)
    sq2_sb = sb("sq2_sb", [128, 2048], BF16, 184320)
    acc_sb = sb("acc_sb", [65, 2048], F32, D + 7296)
    xres = [sb(f"xres_{i}", [128, 2048], F32, D + i * 8192) for i in range(2)]
    woT_sb = sb("woT_sb", [128, 8192], BF16, D + 16384)
    tscratch = sb("tscratch", [128, 2048], F32, 180224)
    tscratch2 = sb("tscratch2", [128, 2048], F32, 151552)
    wqkT_sb = sb("wqkT_sb", [128, 1024], BF16, 180224)
    wq2T_sb = sb("wq2T_sb", [128, 1024], BF16, 182272)
    wk2T_sb = sb("wk2T_sb", [128, 1024], BF16, 184320)
    rows = sb("rows", [65, 4096], F32, 188416)
    CO = 204800
    onescol_bf = sb("c_onescol", [128, 16], BF16, CO)
    onesrow_f32 = sb("c_onesrow", [1, 128], F32, CO + 32)
    onesrow_scaled = sb("c_onesrow_s", [1, 128], F32, CO + 544)
    ones_p64 = sb("c_ones64", [65, 64], F32, CO + 1056)
    ones_p64b = sb("c_ones64b", [65, 64], BF16, D + 15488)
    epscell = sb("c_eps", [1, 8], F32, CO + 1312)
    Eq1_sb = sb("c_Eq1", [128, 1024], BF16, CO + 1344)
    Ek1_sb = sb("c_Ek1", [128, 1024], BF16, CO + 3392)
    E2_sb = sb("c_E2", [128, 1024], BF16, CO + 5440)
    bo_sb = sb("bo_sb", [128, 8], F32, CO + 7488)

    pbig = nc.alloc_psum_tensor("pbig", [128, 2048], F32).ap()
    pbig2 = nc.alloc_psum_tensor("pbig2", [128, 2048], F32).ap()

    S = Sched(nc)
    ld_s = S.sem("ld_s")
    ld_g = S.sem("ld_g")
    ld_x = S.sem("ld_x")
    ld_const = S.sem("ld_const")
    ld_attw = S.sem("ld_attw")
    ld_wv = S.sem("ld_wv")
    ld_wo = S.sem("ld_wo")
    ld_k33 = S.sem("ld_k33")
    ld_mask = S.sem("ld_mask")
    ld_xr = [S.sem("ld_xr0"), S.sem("ld_xr1")]
    ld_gu = [S.sem("ld_gu0"), S.sem("ld_gu1")]
    ld_wd = S.sem("ld_wd")
    ld_c = S.sem("ld_c")
    s_pe = S.sem("s_pe")
    s_dve = S.sem("s_dve")
    s_act = S.sem("s_act")
    s_pool = S.sem("s_pool")
    st = S.sem("st")

    def W(*pairs):
        return list(pairs)

    S.op("sync", lambda e: e.dma_start(out=xf32.rearrange("p (a t) -> p a t", a=8),
                                       in_=xT.rearrange("(a p) t -> p a t", p=128)), inc=ld_x, dma=True)
    x_loaded = ld_x.v
    x_chunk_mark = [x_loaded] * 8
    for dst, src in [(onescol_bf, din["onescol_bf"]), (onesrow_f32, din["onesrow_f32"]),
                     (onesrow_scaled, din["onesrow_scaled"]), (ones_p64, din["ones_p64"]),
                     (epscell, din["epscell"]), (Eq1_sb, din["Eq1"]), (Ek1_sb, din["Ek1"]),
                     (E2_sb, din["E2"])]:
        S.op("sync", lambda e, d=dst, s=src: e.dma_start(out=d, in_=s), inc=ld_const, dma=True)
    consts_loaded = ld_const.v
    for dst, src in [(wvT_sb.rearrange("p (a j) -> p a j", a=8), din["wvT"].rearrange("a p j -> p a j")),
                     (bo_sb, din["bo_t"])]:
        S.op("gpsimd", lambda e, d=dst, s=src: e.dma_start(out=d, in_=s), inc=ld_wv, dma=True)
    wv_loaded = ld_wv.v

    def emit_ln(src_chunks, stat_bf, dst_bf, rstd_row_lhs, first_wait, per_cc_waits=None, psum_gate=0, sqbs=None):
        sq_mark = {0: 0, 1: 0}
        if sqbs is None:
            sqbs = [exsc, q1k1_sb]
        cast_marks = []
        for cc in range(8):
            xb_cc = stat_bf[:, cc * 2048:(cc + 1) * 2048]
            S.op("vector", lambda e, s=src_chunks[cc], d=xb_cc: e.tensor_copy(d, s),
                 waits=W(first_wait, per_cc_waits[cc]) if per_cc_waits else W(first_wait), inc=s_dve)
            cast_done = s_dve.v
            cast_marks.append(cast_done)
            sqb = sqbs[cc % 2]
            S.op("scalar", lambda e, s=xb_cc, d=sqb: e.activation(d, s, AF.Square),
                 waits=W((s_dve, cast_done), (s_pe, sq_mark[cc % 2])), inc=s_act)
            sq_done = s_act.v
            for j in range(4):
                c0, c1 = j * 512, (j + 1) * 512
                S.op("tensor", lambda e, s=xb_cc, c0=c0, c1=c1, f=(cc == 0), l=(cc == 7): e.matmul(
                    pbig[0:1, c0:c1], onescol_bf[:, 0:1], s[:, c0:c1], start=f, stop=l),
                    waits=W((s_dve, cast_done), (ld_const, consts_loaded), (s_dve, psum_gate)), inc=s_pe)
                S.op("tensor", lambda e, sqb=sqb, c0=c0, c1=c1, f=(cc == 0), l=(cc == 7): e.matmul(
                    pbig2[0:1, c0:c1], onescol_bf[:, 0:1], sqb[:, c0:c1], start=f, stop=l),
                    waits=W((s_act, sq_done)), inc=s_pe)
            sq_mark[cc % 2] = s_pe.v
        stats_done = s_pe.v
        nm = rows[0:1, 0:2048]
        rstd = rows[0:1, 2048:4096]
        e2 = rows[32:33, 0:2048]
        ms = rows[32:33, 2048:4096]
        var = rows[64:65, 0:2048]
        sqv = rows[64:65, 2048:4096]
        S.op("scalar", lambda e: e.activation(nm, pbig[0:1, :], AF.Copy, scale=-1.0 / 1024),
             waits=W((s_pe, stats_done)), inc=s_act)
        S.op("scalar", lambda e: e.activation(e2, pbig2[0:1, :], AF.Copy, scale=1.0 / 1024), inc=s_act)
        nm_done = s_act.v
        S.op("scalar", lambda e: e.activation(ms, nm, AF.Square), waits=W((s_act, nm_done)), inc=s_act)
        ms_done = s_act.v
        S.op("vector", lambda e: e.tensor_sub(var, e2, ms), waits=W((s_act, ms_done)), inc=s_dve)
        var_done = s_dve.v
        S.op("scalar", lambda e: e.activation(sqv, var, AF.Ln, bias=epscell[0:1, 0:1]),
             waits=W((s_dve, var_done), (ld_const, consts_loaded)), inc=s_act)
        sqv_done = s_act.v
        S.op("scalar", lambda e: e.activation(rstd, sqv, AF.Exp, scale=-0.5),
             waits=W((s_act, sqv_done)), inc=s_act)
        rstd_done = s_act.v
        for j in range(4):
            c0, c1 = j * 512, (j + 1) * 512
            S.op("tensor", lambda e, c0=c0, c1=c1: e.matmul(
                pbig[:, c0:c1], onesrow_f32, nm[:, c0:c1], start=True, stop=True),
                waits=W((s_act, ms_done)), inc=s_pe)
            S.op("tensor", lambda e, c0=c0, c1=c1: e.matmul(
                pbig2[:, c0:c1], rstd_row_lhs, rstd[:, c0:c1], start=True, stop=True),
                waits=W((s_act, rstd_done)), inc=s_pe)
        bc_done = s_pe.v
        mul_done = [0, 0]
        for cc in range(8):
            ts = tscratch if cc % 2 == 0 else tscratch2
            S.op("vector", lambda e, s=src_chunks[cc], ts=ts: e.tensor_add(ts, s, pbig),
                 waits=W((s_pe, bc_done), (s_dve, mul_done[cc % 2])), inc=s_dve)
            t_done = s_dve.v
            S.op("vector", lambda e, d=dst_bf[:, cc * 2048:(cc + 1) * 2048], ts=ts:
                 e.tensor_mul(d, ts, pbig2), waits=W((s_dve, t_done)), inc=s_dve)
            mul_done[cc % 2] = s_dve.v
        return s_dve.v

    xf32_chunks = [xf32[:, cc * 2048:(cc + 1) * 2048] for cc in range(8)]
    ln1_done = 0
    if "ln1" in phases:
        ln1_done = emit_ln(xf32_chunks, xbf, xn, onesrow_f32, (ld_x, x_loaded))
        if "xn" in taps:
            S.op("gpsimd", lambda e: e.dma_start(out=dtap["xn"], in_=xn),
                 waits=W((s_dve, ln1_done)), inc=ld_c, dma=True)
    for dst, src in [(wqkT_sb.rearrange("p (a j) -> p a j", a=8), din["wqkT"].rearrange("a p j -> p a j")),
                     (wq2T_sb.rearrange("p (a j) -> p a j", a=8), din["wq2T"].rearrange("a p j -> p a j")),
                     (wk2T_sb.rearrange("p (a j) -> p a j", a=8), din["wk2T"].rearrange("a p j -> p a j"))]:
        S.op("gpsimd", lambda e, d=dst, s=src: e.dma_start(out=d, in_=s),
             waits=W((s_dve, ln1_done)), inc=ld_attw, dma=True)
    attw_loaded = ld_attw.v

    proj_act_done = s_act.v
    expans_done = s_dve.v
    k33ones_done = 0
    if "proj" in phases:
        def proj128(wsb, dst_sb, extra_waits, psum):
            for cc in range(8):
                for j in range(4):
                    c0, c1 = j * 512, (j + 1) * 512
                    S.op("tensor", lambda e, cc=cc, c0=c0, c1=c1, w=wsb, p=psum,
                         f=(cc == 0), l=(cc == 7): e.matmul(
                        p[:, c0:c1], w[:, cc * 128:(cc + 1) * 128],
                        xn[:, cc * 2048 + c0: cc * 2048 + c1], start=f, stop=l),
                        waits=W(*extra_waits) if cc == 0 and j == 0 else (), inc=s_pe)
            m = s_pe.v
            S.op("vector", lambda e, d=dst_sb, p=psum: e.tensor_copy(d, p),
                 waits=W((s_pe, m)), inc=s_dve)
            return s_dve.v

        ev1 = proj128(wqkT_sb, q1k1_sb, [(s_dve, ln1_done), (ld_attw, attw_loaded)], pbig)
        ev2 = proj128(wq2T_sb, q2_sb, [(s_dve, ln1_done)], pbig2)
        ev3 = proj128(wk2T_sb, k2_sb, [(s_dve, ev1)], pbig)
        if "q1k1" in taps:
            S.op("gpsimd", lambda e: e.dma_start(out=dtap["q1k1"], in_=q1k1_sb),
                 waits=W((s_dve, ev1)), inc=ld_c, dma=True)
        if "q2" in taps:
            S.op("gpsimd", lambda e: e.dma_start(out=dtap["q2"], in_=q2_sb),
                 waits=W((s_dve, ev2)), inc=ld_c, dma=True)
        if "k2" in taps:
            S.op("gpsimd", lambda e: e.dma_start(out=dtap["k2"], in_=k2_sb),
                 waits=W((s_dve, ev3)), inc=ld_c, dma=True)

        exp_free = {0: 0}
        vhalf_guard = {0: 0, 1: 0}
        S.op("vector", lambda e: e.memset(
            V_aug.rearrange("p (x f) -> p x f", f=65)[:, :, 64:65], 1.0), inc=s_dve)
        vones = s_dve.v
        for c in range(8):
            for j in range(4):
                c0, c1 = j * 512, (j + 1) * 512
                S.op("tensor", lambda e, c=c, c0=c0, c1=c1: e.matmul(
                    pbig2[:, c0:c1], Eq1_sb[0:64, c * 128:(c + 1) * 128], q1k1_sb[0:64, c0:c1],
                    start=True, stop=True),
                    waits=W((s_dve, ev2), (s_dve, ev1), (s_act, exp_free[0]),
                            (ld_const, consts_loaded)) if j == 0 else (),
                    inc=s_pe)
            mq1 = s_pe.v
            S.op("scalar", lambda e: e.activation(exsc, pbig2, AF.Copy),
                 waits=W((s_pe, mq1)), inc=s_act)
            evq = s_act.v
            for j in range(4):
                c0, c1 = j * 512, (j + 1) * 512
                S.op("tensor", lambda e, c=c, c0=c0, c1=c1: e.matmul(
                    pbig2[:, c0:c1], E2_sb[:, c * 128:(c + 1) * 128], q2_sb[:, c0:c1],
                    start=True, stop=True),
                    waits=W((s_act, evq), (s_dve, ev2)) if j == 0 else (), inc=s_pe)
            mq2 = s_pe.v
            S.op("vector", lambda e, c=c: e.tensor_mul(Q33[c], exsc, pbig2),
                 waits=W((s_pe, mq2), (s_act, evq)), inc=s_dve)
            mulq = s_dve.v
            for j in range(4):
                c0, c1 = j * 512, (j + 1) * 512
                S.op("tensor", lambda e, c=c, c0=c0, c1=c1: e.matmul(
                    pbig2[:, c0:c1], Ek1_sb[64:128, c * 128:(c + 1) * 128], q1k1_sb[64:128, c0:c1],
                    start=True, stop=True),
                    waits=W((s_dve, mulq), (s_dve, ev1)) if j == 0 else (), inc=s_pe)
            mk1 = s_pe.v
            S.op("scalar", lambda e: e.activation(exsc, pbig2, AF.Copy),
                 waits=W((s_pe, mk1), (s_dve, mulq)), inc=s_act)
            evk = s_act.v
            for j in range(4):
                c0, c1 = j * 512, (j + 1) * 512
                S.op("tensor", lambda e, c=c, c0=c0, c1=c1: e.matmul(
                    pbig2[:, c0:c1], E2_sb[:, c * 128:(c + 1) * 128], k2_sb[:, c0:c1],
                    start=True, stop=True),
                    waits=W((s_act, evk), (s_dve, ev3)) if j == 0 else (), inc=s_pe)
            mk2 = s_pe.v
            S.op("vector", lambda e, c=c: e.tensor_mul(K33[c], exsc, pbig2),
                 waits=W((s_pe, mk2), (s_act, evk)), inc=s_dve)
            exp_free[0] = s_act.v
            ev2b = s_dve.v
            for ti in (2 * c, 2 * c + 1):
                half = ti % 2
                for cc in range(8):
                    for j in range(2):
                        c0, c1 = half * 1024 + j * 512, half * 1024 + (j + 1) * 512
                        S.op("tensor", lambda e, ti=ti, cc=cc, c0=c0, c1=c1,
                             f=(cc == 0), l=(cc == 7), half=half: e.matmul(
                            pbig[:, c0:c1], xn[:, cc * 2048 + ti * 128: cc * 2048 + (ti + 1) * 128],
                            wvT_sb[:, cc * 1024 + (c0 - half * 1024): cc * 1024 + (c1 - half * 1024)],
                            start=f, stop=l),
                            waits=W((s_act, vhalf_guard[half]), (s_dve, ln1_done),
                                    (ld_wv, wv_loaded), (s_dve, vones)) if cc == 0 and j == 0 else (),
                            inc=s_pe)
                mv = s_pe.v
                S.op("scalar", lambda e, ti=ti, half=half: e.activation(
                    V_aug.rearrange("p (x f) -> p x f", f=65)[:, ti * 16:(ti + 1) * 16, 0:64],
                    pbig[:, half * 1024:(half + 1) * 1024].rearrange("p (h d) -> p h d", d=64), AF.Copy),
                    waits=W((s_pe, mv)), inc=s_act)
                vhalf_guard[half] = s_act.v
        expans_done = s_dve.v
        for c in range(8):
            S.op("sync", lambda e, c=c: e.dma_start(out=K33[c][32:33, :], in_=din["ones_row2048"]),
                 waits=W((s_dve, expans_done)), inc=ld_k33, dma=True)
            S.op("sync", lambda e, c=c: e.dma_start(out=K33[c][96:97, :], in_=din["ones_row2048"]),
                 inc=ld_k33, dma=True)
        k33ones_done = ld_k33.v

        proj_act_done = s_act.v
        S.op("gpsimd", lambda e: e.dma_start(out=woT_sb, in_=din["woT"]),
             waits=W((s_act, proj_act_done)), inc=ld_wo, dma=True)
        woT_loaded = ld_wo.v
        if "Q33" in taps:
            for c in range(8):
                S.op("gpsimd", lambda e, c=c: e.dma_start(
                    out=dtap["Q33"].rearrange("p (a t) -> p a t", a=8)[:, c:c + 1, :], in_=Q33[c]),
                    waits=W((s_dve, expans_done)), inc=ld_c, dma=True)
        if "K33" in taps:
            for c in range(8):
                S.op("gpsimd", lambda e, c=c: e.dma_start(
                    out=dtap["K33"].rearrange("p (a t) -> p a t", a=8)[:, c:c + 1, :], in_=K33[c]),
                    waits=W((s_dve, expans_done), (ld_k33, k33ones_done)), inc=ld_c, dma=True)
        if "Vaug" in taps:
            S.op("gpsimd", lambda e: e.dma_start(out=dtap["Vaug"], in_=V_aug),
                 waits=W((s_act, proj_act_done)), inc=ld_c, dma=True)

    if "proj" not in phases:
        woT_loaded = 0
    attn_done_dve = s_dve.v
    if "attn" in phases:
        S.op("sync", lambda e: e.dma_start(out=trimask_sb, in_=din["trimask"]),
             waits=W((s_dve, expans_done)), inc=ld_mask, dma=True)
        S.op("sync", lambda e: e.dma_start(out=ident_sb, in_=din["ident"]), inc=ld_mask, dma=True)
        S.op("sync", lambda e: e.dma_start(out=ones_p64b, in_=din["ones_p64b"]), inc=ld_mask, dma=True)
        trimask_loaded = ld_mask.v
        A_WINS = [pbig[:, 0:512], pbig[:, 512:1024]]
        S_PAIR = [pbig[:, 1024:2048], pbig2[:, 0:1024]]
        ACC = [pbig2[:, 1024:1536], pbig2[:, 1536:2048]]
        stA = {"red": 0}
        stB = {"exp": [0, 0], "kb": 0}
        chain = {"last_transpose": 0, "last_gather": 0,
                 "acc_free": [0, 0],
                 "gather_mark": {}, "mm": 0}

        def passA_gen(c):
            for qt in range(16):
                keys = (qt + 1) * 128
                for rd in range((keys + 511) // 512):
                    k0 = rd * 512
                    w = min(keys - k0, 512)
                    diag_here = k0 <= qt * 128 < k0 + w
                    for hh in range(2):
                        base = hh * 64
                        S.op("tensor", lambda e, qt=qt, k0=k0, w=w, c=c, base=base, hh=hh,
                             l=(not diag_here): e.matmul(
                            A_WINS[hh][:, 0:w],
                            Q33[c][base:base + 32, qt * 128:(qt + 1) * 128],
                            K33[c][base:base + 32, k0:k0 + w],
                            start=True, stop=l),
                            waits=W((s_dve, stA["red"]), (s_act, proj_act_done)) if hh == 0 else (),
                            inc=s_pe)
                    if diag_here:
                        c0 = qt * 128 - k0
                        for hh in range(2):
                            S.op("tensor", lambda e, c0=c0, hh=hh: e.matmul(
                                A_WINS[hh][:, c0:c0 + 128], ident_sb, trimask_sb,
                                start=False, stop=True),
                                waits=W((ld_mask, trimask_loaded)) if hh == 0 else (), inc=s_pe)
                    mm = s_pe.v
                    for hh in range(2):
                        col = hh * 16 + qt
                        if rd == 0:
                            S.op("vector", lambda e, col=col, w=w, hh=hh: e.reduce_max(
                                Mpos[:, col:col + 1], A_WINS[hh][:, 0:w], axis=AX),
                                waits=W((s_pe, mm)), inc=s_dve)
                        else:
                            S.op("vector", lambda e, w=w, hh=hh: e.reduce_max(
                                tscratch[:, hh:hh + 1], A_WINS[hh][:, 0:w], axis=AX),
                                waits=W((s_pe, mm)), inc=s_dve)
                            rmk = s_dve.v
                            S.op("vector", lambda e, col=col, hh=hh: e.tensor_max(
                                Mpos[:, col:col + 1], Mpos[:, col:col + 1], tscratch[:, hh:hh + 1]),
                                waits=W((s_dve, rmk)), inc=s_dve)
                    stA["red"] = s_dve.v
                    yield

        def passA_chain(c):
            S.op("vector", lambda e: e.tensor_scalar_mul(Mbf[:, 0:32], Mpos, -1.0),
                 waits=W((s_dve, stA["red"]), (ld_c, chain["last_transpose"])), inc=s_dve)
            neg_done = s_dve.v
            S.op("sync", lambda e: e.dma_start(out=Mt, in_=Mbf, transpose=True),
                 waits=W((s_dve, neg_done), (ld_c, chain["last_gather"])), inc=ld_c, dma=True)
            chain["last_transpose"] = ld_c.v
            S.op("sync", lambda e, c=c: e.dma_start(
                out=Q33[c][32:33, :], in_=Mt[0:16, :]),
                waits=W((ld_c, chain["last_transpose"])), inc=ld_c, dma=True)
            S.op("sync", lambda e, c=c: e.dma_start(
                out=Q33[c][96:97, :], in_=Mt[16:32, :]),
                inc=ld_c, dma=True)
            chain["last_gather"] = ld_c.v
            chain["gather_mark"][c] = ld_c.v

        def passB_gen(c):
            gather_mark = chain["gather_mark"][c]
            for qc in range(4):
                nkc = 4 * (qc + 1)
                q0 = qc * 512
                pend = []
                for kc in range(nkc):
                    slot = stB["kb"] % 2
                    stB["kb"] += 1
                    sreg = S_PAIR[slot]
                    for hh in range(2):
                        base = hh * 64
                        S.op("tensor", lambda e, c=c, base=base, kc=kc, sreg=sreg, q0=q0, hh=hh: e.matmul(
                            sreg[:, hh * 512:(hh + 1) * 512],
                            K33[c][base:base + 33, kc * 128:(kc + 1) * 128],
                            Q33[c][base:base + 33, q0:q0 + 512],
                            start=True, stop=True),
                            waits=W((ld_c, gather_mark), (ld_k33, k33ones_done),
                                    (s_act, max(stB["exp"][slot], proj_act_done))) if hh == 0 else (),
                            inc=s_pe)
                    smark = s_pe.v
                    S.op("scalar", lambda e, slot=slot, sreg=sreg: e.activation(
                        Pt[slot], sreg, AF.Exp, scale=SCALE),
                        waits=W((s_pe, smark)), inc=s_act)
                    stB["exp"][slot] = s_act.v
                    emark = s_act.v
                    pool_mark = 0
                    if kc >= 4 * qc:
                        w = (kc - 4 * qc + 1) * 128
                        bb = q0 - kc * 128
                        for hh in range(2):
                            S.op("gpsimd", lambda e, b=slot, w=w, bb=bb, hh=hh: e.affine_select(
                                Pt[b][:, hh * 512: hh * 512 + w], Pt[b][:, hh * 512: hh * 512 + w],
                                [[1, w]], ALU.is_ge, 0.0,
                                base=bb, channel_multiplier=-1),
                                waits=W((s_act, emark)) if hh == 0 else (), inc=s_pool)
                        pool_mark = s_pool.v
                    while len(pend) >= 3:
                        p = pend.pop(0)
                        S.op("tensor", p[0], waits=p[1], inc=s_pe)
                    for hh in range(2):
                        h = 2 * c + hh
                        fn = (lambda e, kc=kc, h=h, b=slot, hh=hh,
                              f=(kc == 0), l=(kc == nkc - 1): e.matmul(
                            ACC[hh][0:65, :],
                            V_aug[:, kc * 1040 + h * 65: kc * 1040 + (h + 1) * 65],
                            Pt[b][:, hh * 512:(hh + 1) * 512], start=f, stop=l))
                        pend.append((fn, W((s_act, emark), (s_pool, pool_mark),
                                           (s_act, chain["acc_free"][hh] if kc == 0 else 0))))
                    yield
                for p in pend:
                    S.op("tensor", p[0], waits=p[1], inc=s_pe)
                pv_done = s_pe.v
                for hh in range(2):
                    astage = acc_sb if hh == 0 else rows[0:65, 2048:4096]
                    S.op("scalar", lambda e, hh=hh, q0=q0, astage=astage: e.activation(
                        astage[:, q0:q0 + 512], ACC[hh][0:65, :], AF.Copy),
                        waits=W((s_pe, pv_done), (s_dve, chain["mm"])) if hh == 0 else (), inc=s_act)
                    chain["acc_free"][hh] = s_act.v
            for hh in range(2):
                base = hh * 64
                astage = acc_sb if hh == 0 else rows[0:65, 2048:4096]
                ev = chain["acc_free"][hh]
                rr = rrb_sb[64:65, 0:2048]
                lnrow = rows[32:33, 0:2048]
                S.op("scalar", lambda e, lnrow=lnrow, astage=astage: e.activation(
                    lnrow, astage[64:65, 0:2048], AF.Ln),
                    waits=W((s_act, ev), (s_dve, chain["mm"])), inc=s_act)
                lnm = s_act.v
                S.op("scalar", lambda e, rr=rr, lnrow=lnrow: e.activation(
                    rr, lnrow, AF.Exp, scale=-1.0),
                    waits=W((s_act, lnm), (s_pe, chain.get("rbc", 0))), inc=s_act)
                rmark = s_act.v
                for qc in range(4):
                    q0 = qc * 512
                    S.op("tensor", lambda e, rr=rr, q0=q0: e.matmul(
                        pbig[0:64, 1024:1536], ones_p64b[64:65, :], rr[:, q0:q0 + 512],
                        start=True, stop=True),
                        waits=W((s_act, rmark), (s_act, stB["exp"][0]),
                                (ld_mask, trimask_loaded)), inc=s_pe)
                    bmark = s_pe.v
                    chain["rbc"] = s_pe.v
                    S.op("scalar", lambda e: e.activation(rb_sb, pbig[0:64, 1024:1536], AF.Copy),
                         waits=W((s_pe, bmark), (s_dve, chain["mm"])), inc=s_act)
                    stB["exp"][0] = s_act.v
                    emark2 = s_act.v
                    S.op("vector", lambda e, c=c, base=base, q0=q0, astage=astage: e.tensor_mul(
                        attn_fm[c][base:base + 64, q0:q0 + 512], astage[0:64, q0:q0 + 512], rb_sb),
                        waits=W((s_act, emark2)), inc=s_dve)
                    chain["mm"] = s_dve.v

        for _ in passA_gen(0):
            pass
        passA_chain(0)
        npair = (n_heads + 1) // 2
        for c in range(npair):
            gb = passB_gen(c)
            ga = passA_gen(c + 1) if c + 1 < npair else None
            while True:
                done_b = next(gb, "end") == "end"
                if ga is not None:
                    next(ga, "end")
                if done_b:
                    break
            if ga is not None:
                for _ in ga:
                    pass
                passA_chain(c + 1)
        attn_done_dve = s_dve.v
    pe_after_attn = s_pe.v
    if "attn" in phases:
        if "attn_fm" in taps:
            for c in range(8):
                S.op("gpsimd", lambda e, c=c: e.dma_start(
                    out=dtap["attn_fm"].rearrange("p (a t) -> p a t", a=8)[:, c:c + 1, :], in_=attn_fm[c]),
                    waits=W((s_dve, attn_done_dve)), inc=ld_c, dma=True)

    resid_done = attn_done_dve
    resid_mark = [attn_done_dve] * 8
    if "wo" in phases:
        xres_mark = [0, 0]
        consume_mark = [0, 0]
        wo_bank = [attn_done_dve, attn_done_dve]
        for m in range(2):
            S.op("sync", lambda e, m=m: e.dma_start(out=xres[m], in_=xT[m * 128:(m + 1) * 128, :]),
                 waits=W((s_dve, attn_done_dve)), inc=ld_xr[m], dma=True)
            xres_mark[m] = ld_xr[m].v
        for m in range(8):
            for qc in range(4):
                bank = (m * 4 + qc) % 2
                q0 = qc * 512
                for cc in range(8):
                    S.op("tensor", lambda e, m=m, cc=cc, q0=q0, bank=bank,
                         f=(cc == 0), l=(cc == 7): e.matmul(
                        pbig[:, bank * 512:(bank + 1) * 512],
                        woT_sb[:, (cc * 8 + m) * 128:(cc * 8 + m + 1) * 128],
                        attn_fm[cc][:, q0:q0 + 512], start=f, stop=l),
                        waits=W((s_dve, attn_done_dve), (s_dve, wo_bank[bank]),
                                (ld_wo, woT_loaded)) if cc == 0 else (), inc=s_pe)
                wmark = s_pe.v
                if m == 0 and qc == 0 and "dbg" in taps:
                    S.op("vector", lambda e: e.tensor_copy(tscratch[:, 0:512], woT_sb[:, 0:512]),
                         waits=W((s_pe, wmark)), inc=s_dve)
                    S.op("vector", lambda e: e.tensor_copy(tscratch[:, 512:1024], attn_fm[0][:, 0:512]),
                         waits=W((ld_xr[0], xres_mark[0])), inc=s_dve)
                    S.op("vector", lambda e: e.tensor_copy(tscratch[:, 1024:1536], woT_sb[:, 4096:4608]),
                         waits=W((ld_wv, wv_loaded)), inc=s_dve)
                    dbg_mark = s_dve.v
                    S.op("gpsimd", lambda e: e.dma_start(out=dtap["dbg"], in_=tscratch),
                         waits=W((s_dve, dbg_mark)), inc=ld_c, dma=True)
                S.op("vector", lambda e, m=m, q0=q0, bank=bank: e.scalar_tensor_tensor(
                    resid[m][:, q0:q0 + 512], pbig[:, bank * 512:(bank + 1) * 512],
                    bo_sb[:, m:m + 1], xres[m % 2][:, q0:q0 + 512],
                    ALU.add, ALU.add),
                    waits=W((s_pe, wmark), (ld_xr[m % 2], xres_mark[m % 2])), inc=s_dve)
                wo_bank[bank] = s_dve.v
            resid_mark[m] = s_dve.v
            consume_mark[m % 2] = s_dve.v
            if m + 2 < 8:
                S.op("sync", lambda e, m=m: e.dma_start(
                    out=xres[m % 2], in_=xT[(m + 2) * 128:(m + 3) * 128, :]),
                    waits=W((s_dve, consume_mark[m % 2])), inc=ld_xr[m % 2], dma=True)
                xres_mark[m % 2] = ld_xr[m % 2].v
        resid_done = s_dve.v
        if "resid2" in taps:
            for m in range(8):
                S.op("gpsimd", lambda e, m=m: e.dma_start(
                    out=dtap["resid2"].rearrange("p (a t) -> p a t", a=8)[:, m:m + 1, :], in_=resid[m]),
                    waits=W((s_dve, resid_done)), inc=ld_c, dma=True)

    ln2_done = resid_done
    if "ln2" in phases:
        ln2_done = emit_ln(resid, ln2stat, xn2, onesrow_scaled, (s_dve, resid_mark[0]),
                           per_cc_waits=[(s_dve, mk_) for mk_ in resid_mark],
                           psum_gate=resid_done, sqbs=[g_s, sq2_sb])
        if "xn2" in taps:
            S.op("gpsimd", lambda e: e.dma_start(out=dtap["xn2"], in_=xn2),
                 waits=W((s_dve, ln2_done)), inc=ld_c, dma=True)

    ffn_done = ln2_done
    if "ffn" in phases:
        wgu_free = [pe_after_attn] * 4
        wd_free = pe_after_attn
        wload = {}
        gate_guard = 0
        up_guard = ln2_done
        down_bank = [resid_done, resid_done]
        h_consumed = 0

        def load_gu(hc):
            par = hc % 2
            S.op("gpsimd", lambda e, hc=hc, par=par: e.dma_start(
                out=wgu[0 + par], in_=din["wgT"][hc]),
                waits=W((s_pe, wgu_free[0 + par])), inc=ld_gu[par], dma=True)
            S.op("gpsimd", lambda e, hc=hc, par=par: e.dma_start(
                out=wgu[2 + par], in_=din["wuT"][hc]),
                waits=W((s_pe, wgu_free[2 + par])), inc=ld_gu[par], dma=True)
            wload[("gu", hc)] = (ld_gu[par], ld_gu[par].v)

        def load_wd(hc):
            S.op("gpsimd", lambda e, hc=hc: e.dma_start(
                out=wd_oct[hc % 8], in_=din["wdT"][hc]),
                waits=W((s_pe, wd_free)), inc=ld_wd, dma=True)
            wload[("wd", hc)] = ld_wd.v

        load_gu(0)
        load_wd(0)
        for oct_ in range(3):
            for hj in range(8):
                hc = oct_ * 8 + hj
                par = hc % 2
                if hc + 1 < 24:
                    load_gu(hc + 1)
                if hj < 7 and hc + 1 < 24:
                    load_wd(hc + 1)
                for half in range(2):
                    for cc in range(8):
                        for j in range(2):
                            t0 = half * 1024 + j * 512
                            S.op("tensor", lambda e, cc=cc, t0=t0, j=j, par=par,
                                 f=(cc == 0), l=(cc == 7): e.matmul(
                                pbig[:, j * 512:(j + 1) * 512],
                                wgu[0 + par][:, cc * 128:(cc + 1) * 128],
                                xn2[:, cc * 2048 + t0: cc * 2048 + t0 + 512], start=f, stop=l),
                                waits=W(wload[("gu", hc)], (s_dve, ln2_done),
                                        (s_act, gate_guard)) if cc == 0 and j == 0 else (), inc=s_pe)
                    gmark = s_pe.v
                    wgu_free[0 + par] = gmark
                    S.op("scalar", lambda e, half=half: e.activation(
                        g_s[:, half * 1024:(half + 1) * 1024], pbig[:, 0:1024], AF.Silu, scale=127.0),
                        waits=W((s_pe, gmark)), inc=s_act)
                    gate_guard = s_act.v
                    for cc in range(8):
                        for j in range(2):
                            t0 = half * 1024 + j * 512
                            S.op("tensor", lambda e, cc=cc, t0=t0, j=j, par=par,
                                 f=(cc == 0), l=(cc == 7): e.matmul(
                                pbig[:, 1024 + j * 512: 1024 + (j + 1) * 512],
                                wgu[2 + par][:, cc * 128:(cc + 1) * 128],
                                xn2[:, cc * 2048 + t0: cc * 2048 + t0 + 512], start=f, stop=l),
                                waits=W((s_dve, up_guard)) if cc == 0 and j == 0 else (), inc=s_pe)
                    umark = s_pe.v
                    wgu_free[2 + par] = umark
                    S.op("vector", lambda e, hj=hj, half=half: e.tensor_mul(
                        h_oct[hj][:, half * 1024:(half + 1) * 1024],
                        g_s[:, half * 1024:(half + 1) * 1024], pbig[:, 1024:2048]),
                        waits=W((s_pe, umark), (s_act, gate_guard), (s_pe, h_consumed)), inc=s_dve)
                    up_guard = s_dve.v
            hoct_done = s_dve.v
            for m in range(8):
                for qc in range(4):
                    bank = (m * 4 + qc) % 2
                    q0 = qc * 512
                    for hj in range(8):
                        S.op("tensor", lambda e, hj=hj, m=m, q0=q0, bank=bank,
                             f=(hj == 0), l=(hj == 7): e.matmul(
                            pbig2[:, bank * 512:(bank + 1) * 512],
                            wd_oct[hj][:, m * 128:(m + 1) * 128], h_oct[hj][:, q0:q0 + 512],
                            start=f, stop=l),
                            waits=W((s_dve, hoct_done), (s_dve, down_bank[bank]),
                                    (ld_wd, wload[("wd", oct_ * 8 + 7)])) if hj == 0 else (), inc=s_pe)
                    dmark = s_pe.v
                    S.op("vector", lambda e, m=m, q0=q0, bank=bank: e.tensor_add(
                        resid[m][:, q0:q0 + 512], resid[m][:, q0:q0 + 512],
                        pbig2[:, bank * 512:(bank + 1) * 512]),
                        waits=W((s_pe, dmark), (s_dve, resid_done)), inc=s_dve)
                    down_bank[bank] = s_dve.v
            wd_free = s_pe.v
            h_consumed = s_pe.v
            if oct_ + 1 < 3:
                load_wd((oct_ + 1) * 8)
        ffn_done = s_dve.v

    S.op("sync", lambda e: e.dma_start(
        out=out.rearrange("(a p) t -> p a t", p=128),
        in_=xf32.rearrange("p (a t) -> p a t", a=8)),
        waits=W((s_dve, ffn_done)), inc=st, dma=True)
    S.op("gpsimd", lambda e: e.engine_nop(),
         waits=W((st, st.v), (ld_c, ld_c.v)))

    S.emit()
    return nc


def kernel(L0_attn_x, L0_attn_Wq1, L0_attn_Wk1, L0_attn_Wq2, L0_attn_Wk2,
           L0_attn_Wv, L0_attn_Wo, L0_attn_bo, L0_ffn_w_gate, L0_ffn_w_up,
           L0_ffn_w_down):
    from concourse.bass_utils import run_bass_kernel_spmd
    x = np.ascontiguousarray(np.asarray(L0_attn_x, np.float32))
    hw = host_weights(L0_attn_Wq1, L0_attn_Wk1, L0_attn_Wq2, L0_attn_Wk2,
                      L0_attn_Wv, L0_attn_Wo, L0_attn_bo,
                      L0_ffn_w_gate, L0_ffn_w_up, L0_ffn_w_down)
    nc = build()
    in_maps = []
    for b in range(8):
        m = {"xT": np.ascontiguousarray(x[b].T)}
        m.update(hw)
        in_maps.append(m)
    res = run_bass_kernel_spmd(nc, in_maps, core_ids=list(range(8)))
    out = np.stack([np.asarray(res.results[b]["out"], np.float32).T for b in range(8)], axis=0)
    return np.ascontiguousarray(out)
